# version 01
# baseline (speedup 1.0000x reference)
import math
from contextlib import ExitStack

import numpy as np
import ml_dtypes

import concourse.bass as bass
import concourse.mybir as mybir
from concourse.bass_utils import run_bass_kernel_spmd

F32 = mybir.dt.float32
BF16 = mybir.dt.bfloat16
I32 = mybir.dt.int32
I8 = mybir.dt.int8
AF = mybir.ActivationFunctionType
ALU = mybir.AluOpType
AX = mybir.AxisListType

BF = ml_dtypes.bfloat16

D = 4096
S = 4096
BATCH = 2
NCORE = 8
TPC = 1024
NH = 32
HD = 128
INTER = 11008
GU = 2 * INTER
NKT = D // 128
EPS = 1e-5
SCALE_MOE = 16.0 / math.sqrt(16.0)
INV_SQRT_HD = 1.0 / math.sqrt(HD)
EXP_MARGIN = 30.0



class Ctr:
    def __init__(self, sem):
        self.sem = sem
        self.v = 0


class Prog:
    ENGINES = ("tensor", "vector", "scalar", "sync", "gpsimd")

    def __init__(self, nc):
        self.nc = nc
        self.ops = {e: [] for e in self.ENGINES}

    def op(self, eng, make, waits=(), inc=None, n=1):
        wl = []
        for w in waits:
            if isinstance(w, Ctr):
                wl.append((w.sem, w.v))
            else:
                c, val = w
                wl.append((c.sem, val))
        tgt = None
        if inc is not None:
            inc.v += n
            tgt = (inc.sem, n)

        def f(e, make=make, wl=tuple(wl), tgt=tgt):
            for smh, val in wl:
                if val > 0:
                    e.wait_ge(smh, val)
            ins = make(e)
            if tgt is not None:
                ins.then_inc(tgt[0], tgt[1])
        self.ops[eng].append(f)

    def barrier(self, eng, waits):
        wl = []
        for w in waits:
            if isinstance(w, Ctr):
                wl.append((w.sem, w.v))
            else:
                c, val = w
                wl.append((c.sem, val))

        def f(e, wl=tuple(wl)):
            for smh, val in wl:
                if val > 0:
                    e.wait_ge(smh, val)
        self.ops[eng].append(f)

    def run(self):
        with self.nc.Block() as block:
            @block.tensor
            def _(t):
                for f in self.ops["tensor"]:
                    f(t)

            @block.vector
            def _(v):
                for f in self.ops["vector"]:
                    f(v)

            @block.scalar
            def _(s):
                for f in self.ops["scalar"]:
                    f(s)

            @block.sync
            def _(s):
                for f in self.ops["sync"]:
                    f(s)

            @block.gpsimd
            def _(g):
                for f in self.ops["gpsimd"]:
                    f(g)


def consts_host():
    E4 = np.zeros((4, 128), dtype=np.float32)
    for g in range(4):
        E4[g, 32 * g:32 * g + 32] = 1
    G4 = np.zeros((128, 4), dtype=BF)
    for g in range(4):
        G4[32 * g:32 * g + 32, g] = 1
    E16 = np.zeros((4, 64), dtype=np.float32)
    for e in range(4):
        E16[e, 16 * e:16 * e + 16] = 1
    return dict(
        onesrow_bf=np.ones((1, 128), dtype=BF),
        onesrow_f=np.ones((1, 128), dtype=np.float32),
        onescol_bf=np.ones((128, 1), dtype=BF),
        onescol_f=np.ones((128, 1), dtype=np.float32),
        ident_f=np.eye(128, dtype=np.float32),
        E4=E4, G4=G4, E16=E16,
    )



def build_prep():
    nc = bass.Bass("TRN2", target_bir_lowering=False, debug=False)
    d = lambda name, shape, dt, kind: nc.dram_tensor(name, shape, dt, kind=kind).ap()
    w0t = d("w0t_s", [512, D], F32, "ExternalInput")
    m0t = d("m0t_s", [512, D], I32, "ExternalInput")
    qgu = d("qgu_s", [512, 2752], I32, "ExternalInput")
    zgu = d("zgu_s", [4, 2752], I32, "ExternalInput")
    sgu = d("sgu_s", [4, GU], F32, "ExternalInput")
    qdn = d("qdn_s", [INTER, 64], I32, "ExternalInput")
    zdn = d("zdn_s", [86, 64], I32, "ExternalInput")
    sdn = d("sdn_s", [86, 512], F32, "ExternalInput")
    wqt = d("wqt_s", [512, D], F32, "ExternalInput")
    wkt = d("wkt_s", [512, D], F32, "ExternalInput")
    wvt = d("wvt_s", [512, D], F32, "ExternalInput")
    onesrow_bf = d("onesrow_bf", [1, 128], BF16, "ExternalInput")
    onesrow_f = d("onesrow_f", [1, 128], F32, "ExternalInput")
    G4d = d("G4", [128, 4], BF16, "ExternalInput")
    E4d = d("E4", [4, 128], F32, "ExternalInput")

    wmt_o = d("wmt_o", [512, D], BF16, "ExternalOutput")
    wgu_o = d("wgu_o", [512, GU], BF16, "ExternalOutput")
    wdn_o = d("wdn_o", [INTER, 512], BF16, "ExternalOutput")
    wqt_o = d("wqt_o", [512, D], BF16, "ExternalOutput")
    wkt_o = d("wkt_o", [512, D], BF16, "ExternalOutput")
    wvt_o = d("wvt_o", [512, D], BF16, "ExternalOutput")

    p = Prog(nc)
    with ExitStack() as ctx:
        sb = lambda name, shape, dt: ctx.enter_context(nc.sbuf_tensor(name, shape, dt))
        psum = lambda name, shape: ctx.enter_context(nc.psum_tensor(name, shape, F32))
        C = lambda name: Ctr(ctx.enter_context(nc.semaphore(name)))

        cl = C("cl")
        ml = C("ml")
        ql = C("ql")
        zl = C("zl")
        sl = C("sl")
        gl = C("gl")
        stc = C("stc")
        stw = C("stw")
        sg0 = C("sg0")
        sg1 = C("sg1")
        te = C("te")
        ve = C("ve")

        ones_r = sb("ones_r", [1, 128], BF16)
        ones_rf = sb("ones_rf", [1, 128], F32)
        G4t = sb("G4t", [128, 4], BF16)
        E4t = sb("E4t", [4, 128], F32)
        p.op("sync", lambda e: e.dma_start(out=ones_r[:, :], in_=onesrow_bf), inc=cl, n=16)
        p.op("sync", lambda e: e.dma_start(out=ones_rf[:, :], in_=onesrow_f), inc=cl, n=16)
        p.op("sync", lambda e: e.dma_start(out=G4t[:, :], in_=G4d), inc=cl, n=16)
        p.op("sync", lambda e: e.dma_start(out=E4t[:, :], in_=E4d), inc=cl, n=16)
        v_consts = (cl, cl.v)

        wcast = sb("wcast", [128, D], BF16)
        w0bf = sb("w0bf", [128, D], BF16)
        for wi, (src, dst) in enumerate([(wqt, wqt_o), (wkt, wkt_o), (wvt, wvt_o)]):
            for kt in range(4):
                vload = gl.v + 16
                p.op("gpsimd",
                     lambda e, src=src, kt=kt: e.dma_start(
                         out=wcast[:, :], in_=src[128 * kt:128 * kt + 128, :]),
                     waits=[(stc, stc.v)], inc=gl, n=16)
                p.op("sync",
                     lambda e, dst=dst, kt=kt: e.dma_start(
                         out=dst[128 * kt:128 * kt + 128, :], in_=wcast[:, :]),
                     waits=[(gl, vload)], inc=stc, n=16)

        m0sb = sb("m0sb", [128, D], I32)
        s_f = sb("s_f", [128, 128], F32)
        s_bf = sb("s_bf", [128, 128], BF16)
        bmsum = psum("bmsum", [4, 128])
        bmthr = sb("bmthr", [4, 128], F32)
        bmE_ps = psum("bmE_ps", [128, 128])
        bmE = sb("bmE", [128, 128], F32)
        wm_sb = sb("wm_sb", [128, D], BF16)
        for kt in range(4):
            v_w0 = (gl, gl.v + 16)
            p.op("gpsimd",
                 lambda e, kt=kt: e.dma_start(
                     out=w0bf[:, :], in_=w0t[128 * kt:128 * kt + 128, :]),
                 waits=[(ve, ve.v)], inc=gl, n=16)
            vload = ml.v + 16
            p.op("sync",
                 lambda e, kt=kt: e.dma_start(out=m0sb[:, :],
                                              in_=m0t[128 * kt:128 * kt + 128, :]),
                 waits=[(ve, ve.v)], inc=ml, n=16)
            p.op("vector",
                 lambda e: e.tensor_reduce(
                     s_f[:, :], m0sb[:, :].rearrange("p (nb w) -> p nb w", w=32),
                     axis=AX.X, op=ALU.add),
                 waits=[(ml, vload), v_consts], inc=ve)
            p.op("vector", lambda e: e.tensor_copy(s_bf[:, :], s_f[:, :]), inc=ve)
            v_s = (ve, ve.v)
            p.op("tensor",
                 lambda e: e.matmul(bmsum[:, :], G4t[:, :], s_bf[:, :],
                                    start=True, stop=True),
                 waits=[v_s, (ve, ve.v)], inc=te)
            p.op("vector",
                 lambda e: e.tensor_scalar(bmthr[:, :], bmsum[:, :], 0.0, None,
                                           op0=ALU.is_gt),
                 waits=[(te, te.v)], inc=ve)
            p.op("tensor",
                 lambda e: e.matmul(bmE_ps[:, :], E4t[:, :], bmthr[:, :],
                                    start=True, stop=True),
                 waits=[(ve, ve.v)], inc=te)
            p.op("vector",
                 lambda e: e.tensor_copy(bmE[:, :], bmE_ps[:, :]),
                 waits=[(te, te.v)], inc=ve)
            vready = (ve, ve.v)
            for nb in range(128):
                p.op("vector",
                     lambda e, kt=kt, nb=nb: e.tensor_scalar(
                         wm_sb[:, 32 * nb:32 * nb + 32],
                         w0bf[:, 32 * nb:32 * nb + 32],
                         bmE[:, nb:nb + 1], None, op0=ALU.mult),
                     waits=[vready, v_w0] if nb == 0 else [],
                     inc=ve if nb == 127 else None)
            v_wm = (ve, ve.v)
            p.op("sync",
                 lambda e, kt=kt: e.dma_start(
                     out=wmt_o[128 * kt:128 * kt + 128, :], in_=wm_sb[:, :]),
                 waits=[v_wm], inc=stw, n=16)
            p.barrier("vector", [(stw, stw.v)])

        qgu_sb = sb("qgu_sb", [128, 2752], I32)
        nib8 = sb("nib8", [128, GU], I8)
        zrow = sb("zrow", [1, 2752], I32)
        zn8 = sb("zn8", [1, GU], I8)
        znbf = sb("znbf", [1, 11 * 512], BF16)
        srow = sb("srow", [1, 11 * 512], F32)
        zb_ps = psum("zb_ps", [128, 512])
        sb_ps = psum("sb_ps", [128, 512])
        zb_sb = sb("zb_sb", [128, 512], BF16)
        sb_sb = sb("sb_sb", [128, 512], BF16)
        w1_sb = sb("w1_sb", [128, 512], BF16)
        wout = sb("wout", [128, 1024], BF16)
        for kc in range(4):
            vq = ql.v + 16
            p.op("sync", lambda e, kc=kc: e.dma_start(
                out=qgu_sb[:, :], in_=qgu[128 * kc:128 * kc + 128, :]),
                waits=[(ve, ve.v)], inc=ql, n=16)
            vz = zl.v + 16
            p.op("sync", lambda e, kc=kc: e.dma_start(
                out=zrow[:, :], in_=zgu[kc:kc + 1, :]),
                waits=[(ve, ve.v)], inc=zl, n=16)
            p.op("vector", lambda e: e.tensor_scalar(
                nib8[:, 0:INTER], qgu_sb[:, :].bitcast(I8), 15, None,
                op0=ALU.bitwise_and), waits=[(ql, vq)], inc=ve)
            p.op("vector", lambda e: e.tensor_scalar(
                nib8[:, INTER:GU], qgu_sb[:, :].bitcast(I8), 4, 15,
                op0=ALU.logical_shift_right, op1=ALU.bitwise_and), inc=ve)
            p.op("vector", lambda e: e.tensor_scalar(
                zn8[:, 0:INTER], zrow[:, :].bitcast(I8), 15, None,
                op0=ALU.bitwise_and), waits=[(zl, vz)], inc=ve)
            p.op("vector", lambda e: e.tensor_scalar(
                zn8[:, INTER:GU], zrow[:, :].bitcast(I8), 4, 15,
                op0=ALU.logical_shift_right, op1=ALU.bitwise_and), inc=ve)
            v_unp = (ve, ve.v)
            nib_il = nib8[:, :].rearrange("p (pl q) -> p q pl", pl=2)
            zn_il = zn8[:, :].rearrange("p (pl q) -> p q pl", pl=2)
            for grp in range(4):
                g0 = 11 * grp
                ng = min(11, 43 - g0)
                glen = 512 * ng
                vs = sl.v + 16
                p.op("sync", lambda e, kc=kc, g0=g0, glen=glen: e.dma_start(
                    out=srow[:, 0:glen], in_=sgu[kc:kc + 1, 512 * g0:512 * g0 + glen]),
                    waits=[(te, te.v)], inc=sl, n=16)
                p.op("vector", lambda e, g0=g0, glen=glen: e.tensor_copy(
                    znbf[:, 0:glen].rearrange("p (q pl) -> p q pl", pl=2),
                    zn_il[:, 256 * g0:256 * g0 + glen // 2, :]),
                    waits=[v_unp, (te, te.v)], inc=ve)
                v_zc = (ve, ve.v)
                for j in range(ng):
                    nn = g0 + j
                    o = 512 * nn
                    half = wout[:, 512 * (nn % 2):512 * (nn % 2) + 512]
                    p.op("tensor", lambda e, j=j: e.matmul(
                        zb_ps[:, :], ones_r[:, :], znbf[:, 512 * j:512 * j + 512],
                        start=True, stop=True),
                        waits=[v_zc, (ve, ve.v)], inc=te)
                    p.op("tensor", lambda e, j=j: e.matmul(
                        sb_ps[:, :], ones_rf[:, :], srow[:, 512 * j:512 * j + 512],
                        start=True, stop=True),
                        waits=[(sl, vs)], inc=te)
                    p.op("vector", lambda e: e.tensor_copy(zb_sb[:, :], zb_ps[:, :]),
                         waits=[(te, te.v - 1)], inc=ve)
                    p.op("vector", lambda e: e.tensor_copy(sb_sb[:, :], sb_ps[:, :]),
                         waits=[(te, te.v)], inc=ve)
                    p.op("vector", lambda e, o=o: e.tensor_tensor(
                        w1_sb[:, :].rearrange("p (q pl) -> p q pl", pl=2),
                        nib_il[:, o // 2:o // 2 + 256, :],
                        zb_sb[:, :].rearrange("p (q pl) -> p q pl", pl=2),
                        op=ALU.subtract), inc=ve)
                    hs = sg0 if nn % 2 == 0 else sg1
                    p.op("vector", lambda e, half=half: e.tensor_tensor(
                        half[:, :], w1_sb[:, :], sb_sb[:, :], op=ALU.mult),
                        waits=[(hs, hs.v)], inc=ve)
                    p.op("sync", lambda e, kc=kc, o=o, half=half: e.dma_start(
                        out=wgu_o[128 * kc:128 * kc + 128, o:o + 512],
                        in_=half[:, :]),
                        waits=[(ve, ve.v)], inc=hs, n=16)
        p.barrier("vector", [(sg0, sg0.v), (sg1, sg1.v)])

        qdn_sb = sb("qdn_sb", [128, 64], I32)
        dnib8 = sb("dnib8", [128, 512], I8)
        dzrow = sb("dzrow", [1, 64], I32)
        dzn8 = sb("dzn8", [1, 512], I8)
        dznbf = sb("dznbf", [1, 512], BF16)
        dsrow = sb("dsrow", [1, 512], F32)
        dw1 = sb("dw1", [128, 512], BF16)
        dwout = sb("dwout", [128, 1024], BF16)
        for g in range(86):
            vq = ql.v + 16
            p.op("sync", lambda e, g=g: e.dma_start(
                out=qdn_sb[:, :], in_=qdn[128 * g:128 * g + 128, :]),
                waits=[(ve, ve.v)], inc=ql, n=16)
            vz = zl.v + 16
            p.op("sync", lambda e, g=g: e.dma_start(
                out=dzrow[:, :], in_=zdn[g:g + 1, :]),
                waits=[(ve, ve.v)], inc=zl, n=16)
            vs = sl.v + 16
            p.op("sync", lambda e, g=g: e.dma_start(
                out=dsrow[:, :], in_=sdn[g:g + 1, :]),
                waits=[(ve, ve.v), (te, te.v)], inc=sl, n=16)
            p.op("vector", lambda e: e.tensor_scalar(
                dnib8[:, 0:256], qdn_sb[:, :].bitcast(I8), 15, None,
                op0=ALU.bitwise_and), waits=[(ql, vq)], inc=ve)
            p.op("vector", lambda e: e.tensor_scalar(
                dnib8[:, 256:512], qdn_sb[:, :].bitcast(I8), 4, 15,
                op0=ALU.logical_shift_right, op1=ALU.bitwise_and), inc=ve)
            p.op("vector", lambda e: e.tensor_scalar(
                dzn8[:, 0:256], dzrow[:, :].bitcast(I8), 15, None,
                op0=ALU.bitwise_and), waits=[(zl, vz), (te, te.v)], inc=ve)
            p.op("vector", lambda e: e.tensor_scalar(
                dzn8[:, 256:512], dzrow[:, :].bitcast(I8), 4, 15,
                op0=ALU.logical_shift_right, op1=ALU.bitwise_and), inc=ve)
            p.op("vector", lambda e: e.tensor_copy(
                dznbf[:, :].rearrange("p (q pl) -> p q pl", pl=2),
                dzn8[:, :].rearrange("p (pl q) -> p q pl", pl=2)), inc=ve)
            p.op("tensor", lambda e: e.matmul(
                zb_ps[:, :], ones_r[:, :], dznbf[:, :], start=True, stop=True),
                waits=[(ve, ve.v)], inc=te)
            p.op("tensor", lambda e: e.matmul(
                sb_ps[:, :], ones_rf[:, :], dsrow[:, :], start=True, stop=True),
                waits=[(sl, vs)], inc=te)
            p.op("vector", lambda e: e.tensor_copy(zb_sb[:, :], zb_ps[:, :]),
                 waits=[(te, te.v - 1)], inc=ve)
            p.op("vector", lambda e: e.tensor_copy(sb_sb[:, :], sb_ps[:, :]),
                 waits=[(te, te.v)], inc=ve)
            p.op("vector", lambda e: e.tensor_tensor(
                dw1[:, :].rearrange("p (q pl) -> p q pl", pl=2),
                dnib8[:, :].rearrange("p (pl q) -> p q pl", pl=2),
                zb_sb[:, :].rearrange("p (q pl) -> p q pl", pl=2),
                op=ALU.subtract), inc=ve)
            half = dwout[:, 512 * (g % 2):512 * (g % 2) + 512]
            hs = sg0 if g % 2 == 0 else sg1
            p.op("vector", lambda e, half=half: e.tensor_tensor(
                half[:, :], dw1[:, :], sb_sb[:, :], op=ALU.mult),
                waits=[(hs, hs.v)], inc=ve)
            p.op("sync", lambda e, g=g, half=half: e.dma_start(
                out=wdn_o[128 * g:128 * g + 128, :], in_=half[:, :]),
                waits=[(ve, ve.v)], inc=hs, n=16)

        p.barrier("sync", [(sg0, sg0.v), (sg1, sg1.v), (stw, stw.v),
                           (stc, stc.v), (gl, gl.v)])
        p.run()
    return nc



def shard_rows(a, core, rows):
    return np.ascontiguousarray(a[rows * core:rows * (core + 1)])


def prep_in_maps(inputs, cs):
    w0T = np.ascontiguousarray(inputs["L0_attn_weight"].T)
    m0T = np.ascontiguousarray(inputs["L0_attn_mask"].T)
    wqT = np.ascontiguousarray(inputs["L1_attn_wq"].T)
    wkT = np.ascontiguousarray(inputs["L1_attn_wk"].T)
    wvT = np.ascontiguousarray(inputs["L1_attn_wv"].T)
    qgu = inputs["L1_ffn_qweight_gu"]
    zgu = inputs["L1_ffn_qzeros_gu"]
    sgu = inputs["L1_ffn_scales_gu"]
    qdn = inputs["L1_ffn_qweight_dn"]
    zdn = inputs["L1_ffn_qzeros_dn"]
    sdn = inputs["L1_ffn_scales_dn"]
    maps = []
    for c in range(NCORE):
        maps.append({
            "w0t_s": shard_rows(w0T, c, 512),
            "m0t_s": shard_rows(m0T, c, 512),
            "qgu_s": shard_rows(qgu, c, 512),
            "zgu_s": shard_rows(zgu, c, 4),
            "sgu_s": shard_rows(sgu, c, 4),
            "qdn_s": np.ascontiguousarray(qdn[:, 64 * c:64 * c + 64]),
            "zdn_s": np.ascontiguousarray(zdn[:, 64 * c:64 * c + 64]),
            "sdn_s": np.ascontiguousarray(sdn[:, 512 * c:512 * c + 512]),
            "wqt_s": shard_rows(wqT, c, 512),
            "wkt_s": shard_rows(wkT, c, 512),
            "wvt_s": shard_rows(wvT, c, 512),
            "onesrow_bf": cs["onesrow_bf"],
            "onesrow_f": cs["onesrow_f"],
            "G4": cs["G4"],
            "E4": cs["E4"],
        })
    return maps


def run_prep(inputs):
    cs = consts_host()
    nc = build_prep()
    res = run_bass_kernel_spmd(nc, prep_in_maps(inputs, cs), core_ids=list(range(NCORE)))
    r = res.results
    wmT = np.concatenate([r[c]["wmt_o"] for c in range(NCORE)], axis=0)
    wgu = np.concatenate([r[c]["wgu_o"] for c in range(NCORE)], axis=0)
    wdn = np.concatenate([r[c]["wdn_o"] for c in range(NCORE)], axis=1)
    wqT = np.concatenate([r[c]["wqt_o"] for c in range(NCORE)], axis=0)
    wkT = np.concatenate([r[c]["wkt_o"] for c in range(NCORE)], axis=0)
    wvT = np.concatenate([r[c]["wvt_o"] for c in range(NCORE)], axis=0)
    return dict(wmT=wmT, wgu=wgu, wdn=wdn, wqT=wqT, wkT=wkT, wvT=wvT)


def kernel(**inputs):
    prep = run_prep(inputs)
    x2T = run_l0(inputs, prep["wmT"])
    qT, kT, vT = run_qkv(inputs, x2T, prep["wqT"], prep["wkT"], prep["wvT"])
    x3T = run_attn(x2T, qT, kT, vT)
    outT = run_ffn(x3T, prep["wgu"], prep["wdn"])
    return np.ascontiguousarray(outT.T.astype(np.float32)).reshape(2, S, D)



def ln_rows(p, ctrs, eng_tiles, s1_ps, s2_ps, v_stats, bcast=True):
    ve, se, te = ctrs["ve"], ctrs["se"], ctrs["te"]
    t = eng_tiles
    inv_d = 1.0 / D
    p.op("vector", lambda e: e.tensor_scalar(
        t["m_row"][:, :], s1_ps[:, :], inv_d, None, op0=ALU.mult),
        waits=[v_stats], inc=ve)
    p.op("scalar", lambda e: e.activation(
        t["var_row"][:, :], t["m_row"][:, :], AF.Square),
        waits=[(ve, ve.v)], inc=se)
    p.op("vector", lambda e: e.scalar_tensor_tensor(
        t["var_row"][:, :], s2_ps[:, :], inv_d, t["var_row"][:, :],
        op0=ALU.mult, op1=ALU.subtract),
        waits=[v_stats, (se, se.v)], inc=ve)
    p.op("vector", lambda e: e.tensor_scalar(
        t["var_row"][:, :], t["var_row"][:, :], EPS, None, op0=ALU.add),
        waits=[(ve, ve.v)], inc=ve)
    p.op("scalar", lambda e: e.activation(
        t["rstd_row"][:, :], t["var_row"][:, :], AF.Ln),
        waits=[(ve, ve.v)], inc=se)
    p.op("scalar", lambda e: e.activation(
        t["rstd_row"][:, :], t["rstd_row"][:, :], AF.Exp, scale=-0.5),
        waits=[(se, se.v)], inc=se)
    if not bcast:
        return (se, se.v)
    for half in range(2):
        p.op("tensor", lambda e, half=half: e.matmul(
            t["bc_ps"][:, 512 * half:512 * half + 512], t["ones_rf"][:, :],
            t["m_row"][:, 512 * half:512 * half + 512], start=True, stop=True),
            waits=[(ve, ve.v), (se, se.v)], inc=te)
    p.op("vector", lambda e: e.tensor_copy(t["mB"][:, :], t["bc_ps"][:, :]),
         waits=[(te, te.v)], inc=ve)
    v_mB = (ve, ve.v)
    for half in range(2):
        p.op("tensor", lambda e, half=half: e.matmul(
            t["bc_ps"][:, 512 * half:512 * half + 512], t["ones_rf"][:, :],
            t["rstd_row"][:, 512 * half:512 * half + 512], start=True, stop=True),
            waits=[(se, se.v), v_mB], inc=te)
    p.op("vector", lambda e: e.tensor_copy(t["rstdB"][:, :], t["bc_ps"][:, :]),
         waits=[(te, te.v)], inc=ve)
    return (ve, ve.v)


def build_l0():
    nc = bass.Bass("TRN2", target_bir_lowering=False, debug=False)
    d = lambda name, shape, dt, kind: nc.dram_tensor(name, shape, dt, kind=kind).ap()
    xT = d("xT_s", [D, TPC], F32, "ExternalInput")
    wmT = d("wmT", [D, D], BF16, "ExternalInput")
    b0 = d("b0", [128, 32], F32, "ExternalInput")
    a_re = d("a_re", [D, 64], F32, "ExternalInput")
    b_re = d("b_re", [64, D], F32, "ExternalInput")
    wrT = d("wrT", [D, 4], F32, "ExternalInput")
    E16d = d("E16", [4, 64], F32, "ExternalInput")
    identd = d("ident_f", [128, 128], F32, "ExternalInput")
    onesrow_f = d("onesrow_f", [1, 128], F32, "ExternalInput")
    onescol_f = d("onescol_f", [128, 1], F32, "ExternalInput")
    x2_o = d("x2_o", [D, TPC], F32, "ExternalOutput")
    x1_d = d("x1_d", [D, TPC], F32, "Internal")

    p = Prog(nc)
    with ExitStack() as ctx:
        sb = lambda name, shape, dt: ctx.enter_context(nc.sbuf_tensor(name, shape, dt))
        psum = lambda name, shape, dt=F32: ctx.enter_context(nc.psum_tensor(name, shape, dt))
        C = lambda name: Ctr(ctx.enter_context(nc.semaphore(name)))
        ctrs = {n: C(n) for n in ("ve", "se", "te")}
        ve, se, te = ctrs["ve"], ctrs["se"], ctrs["te"]
        cl = C("cl")
        xl0 = C("xl0")
        xl1 = C("xl1")
        wl0 = C("wl0")
        wl1 = C("wl1")
        so0 = C("so0")
        so1 = C("so1")

        ones_rf = sb("ones_rf", [1, 128], F32)
        ones_cf = sb("ones_cf", [128, 1], F32)
        E16t = sb("E16t", [4, 64], F32)
        id_f = sb("id_f", [128, 128], F32)
        b0_sb = sb("b0_sb", [128, 32], F32)
        wr_bf = sb("wr_bf", [128, 32 * 4], BF16)
        ar_bf = sb("ar_bf", [128, 32 * 64], BF16)
        br_bf = sb("br_bf", [64, D], BF16)
        xt0 = sb("xt0", [128, TPC], F32)
        xt1 = sb("xt1", [128, TPC], F32)
        xsq = sb("xsq", [128, TPC], F32)
        m_row = sb("m_row", [1, TPC], F32)
        var_row = sb("var_row", [1, TPC], F32)
        rstd_row = sb("rstd_row", [1, TPC], F32)
        mB = sb("mB", [128, TPC], F32)
        rstdB = sb("rstdB", [128, TPC], F32)
        ln_sb = sb("ln_sb", [128, NKT * TPC], BF16)
        wslab0 = sb("wslab0", [128, NKT * 128], BF16)
        wslab1 = sb("wslab1", [128, NKT * 128], BF16)
        res0 = sb("res0", [128, TPC], F32)
        res1 = sb("res1", [128, TPC], F32)
        lg_sb = sb("lg_sb", [4, TPC], F32)
        lgT_sb = sb("lgT_sb", [128, 4], F32)
        g_m1 = sb("g_m1", [128, 1], F32)
        g_nm1 = sb("g_nm1", [128, 1], F32)
        g_eq = sb("g_eq", [128, 4], F32)
        g_t2 = sb("g_t2", [128, 4], F32)
        g_m2 = sb("g_m2", [128, 1], F32)
        g_mask = sb("g_mask", [128, 4], F32)
        g_e = sb("g_e", [128, 4], F32)
        g_den = sb("g_den", [128, 1], F32)
        g_rec = sb("g_rec", [128, 1], F32)
        g_out = sb("g_out", [128, 4], F32)
        grow = sb("grow", [4, TPC], F32)
        gE_sb = sb("gE_sb", [64, TPC], F32)
        mid_sb = sb("mid_sb", [64, TPC], BF16)

        s1_ps_full = psum("s1_ps", [64, TPC])
        s2_ps_full = psum("s2_ps", [64, TPC])
        s1_ps = s1_ps_full[0:1, :]
        s2_ps = s2_ps_full[0:1, :]
        yA = psum("yA", [128, TPC])
        yB = psum("yB", [128, TPC])

        for dst, srcd in [(ones_rf, onesrow_f), (ones_cf, onescol_f),
                          (E16t, E16d), (id_f, identd)]:
            p.op("sync", lambda e, dst=dst, srcd=srcd: e.dma_start(
                out=dst[:, :], in_=srcd), inc=cl, n=16)
        p.op("sync", lambda e: e.dma_start(out=b0_sb[:, :], in_=b0), inc=cl, n=16)
        p.op("gpsimd", lambda e: e.dma_start(
            out=wr_bf[:, :].rearrange("p (t c) -> p t c", c=4),
            in_=wrT.rearrange("(t p) c -> p t c", p=128)), inc=cl, n=16)
        p.op("gpsimd", lambda e: e.dma_start(
            out=ar_bf[:, :].rearrange("p (t c) -> p t c", c=64),
            in_=a_re.rearrange("(t p) c -> p t c", p=128)), inc=cl, n=16)
        p.op("gpsimd", lambda e: e.dma_start(out=br_bf[:, :], in_=b_re), inc=cl, n=16)
        v_consts = (cl, cl.v)

        ln_tiles = dict(m_row=m_row, var_row=var_row, rstd_row=rstd_row,
                        mB=mB, rstdB=rstdB, bc_ps=yA, ones_rf=ones_rf)

        def stats_pass(src_dram, v_extra=None):
            for kt in range(NKT):
                xl = xl0 if kt % 2 == 0 else xl1
                xt = xt0 if kt % 2 == 0 else xt1
                vx = xl.v + 16
                p.op("sync", lambda e, kt=kt, xt=xt: e.dma_start(
                    out=xt[:, :], in_=src_dram[128 * kt:128 * kt + 128, :]),
                    waits=[(te, te.v), (se, se.v)], inc=xl, n=16)
                p.op("scalar", lambda e, xt=xt: e.activation(
                    xsq[:, :], xt[:, :], AF.Square),
                    waits=[(xl, vx), (te, te.v)], inc=se)
                for half in range(2):
                    p.op("tensor", lambda e, xt=xt, half=half, kt=kt: e.matmul(
                        s1_ps[:, 512 * half:512 * half + 512], ones_cf[:, :],
                        xt[:, 512 * half:512 * half + 512],
                        start=(kt == 0), stop=(kt == NKT - 1)),
                        waits=[(xl, vx)] + ([v_consts] if kt == 0 else []), inc=te)
                for half in range(2):
                    p.op("tensor", lambda e, half=half, kt=kt: e.matmul(
                        s2_ps[:, 512 * half:512 * half + 512], ones_cf[:, :],
                        xsq[:, 512 * half:512 * half + 512],
                        start=(kt == 0), stop=(kt == NKT - 1)),
                        waits=[(se, se.v)], inc=te)
            return (te, te.v)

        def materialize_pass(src_dram, v_rows, extra_waits=()):
            for kt in range(NKT):
                xl = xl0 if kt % 2 == 0 else xl1
                xt = xt0 if kt % 2 == 0 else xt1
                vx = xl.v + 16
                p.op("sync", lambda e, kt=kt, xt=xt: e.dma_start(
                    out=xt[:, :], in_=src_dram[128 * kt:128 * kt + 128, :]),
                    waits=[(ve, ve.v), (se, se.v)] + (list(extra_waits) if kt == 0 else []),
                    inc=xl, n=16)
                p.op("vector", lambda e, xt=xt: e.tensor_sub(
                    xsq[:, :], xt[:, :], mB[:, :]),
                    waits=[(xl, vx), v_rows, (se, se.v)], inc=ve)
                p.op("vector", lambda e, kt=kt: e.tensor_mul(
                    ln_sb[:, TPC * kt:TPC * kt + TPC], xsq[:, :], rstdB[:, :]),
                    waits=[(ve, ve.v)], inc=ve)
            return (ve, ve.v)

        v_st = stats_pass(xT)
        v_rows = ln_rows(p, ctrs, ln_tiles, s1_ps, s2_ps, v_st)
        v_ln0 = materialize_pass(xT, v_rows)

        for nt in range(NKT):
            wl = wl0 if nt % 2 == 0 else wl1
            ws = wslab0 if nt % 2 == 0 else wslab1
            y = yA if nt % 2 == 0 else yB
            so = so0 if nt % 2 == 0 else so1
            res = res0 if nt % 2 == 0 else res1
            vw = wl.v + 16
            p.op("sync", lambda e, nt=nt, ws=ws: e.dma_start(
                out=ws[:, :].rearrange("p (kt c) -> p kt c", c=128),
                in_=wmT.rearrange("(kt p) n -> p kt n", p=128)[
                    :, :, 128 * nt:128 * nt + 128]),
                waits=[(te, te.v)], inc=wl, n=16)
            xl = xl0 if nt % 2 == 0 else xl1
            xt = xt0 if nt % 2 == 0 else xt1
            vx = xl.v + 16
            p.op("sync", lambda e, nt=nt, xt=xt: e.dma_start(
                out=xt[:, :], in_=xT[128 * nt:128 * nt + 128, :]),
                waits=[(ve, ve.v), (se, se.v)], inc=xl, n=16)
            for k in range(NKT):
                for half in range(2):
                    p.op("tensor", lambda e, k=k, half=half, nt=nt, ws=ws, y=y: e.matmul(
                        y[:, 512 * half:512 * half + 512],
                        ws[:, 128 * k:128 * k + 128],
                        ln_sb[:, TPC * k + 512 * half:TPC * k + 512 * half + 512],
                        start=(k == 0), stop=(k == NKT - 1)),
                        waits=([(wl, vw), v_ln0, (ve, ve.v)] if k == 0 and half == 0 else []),
                        inc=te if (k == NKT - 1 and half == 1) else None)
            v_y = (te, te.v)
            p.op("vector", lambda e, nt=nt, y=y, xt=xt, res=res: e.scalar_tensor_tensor(
                res[:, :], y[:, :], b0_sb[:, nt:nt + 1], xt[:, :],
                op0=ALU.add, op1=ALU.add),
                waits=[v_y, (xl, vx), (so, so.v)], inc=ve)
            v_x1 = (ve, ve.v)
            p.op("scalar", lambda e, res=res: e.activation(
                xsq[:, :], res[:, :], AF.Square),
                waits=[v_x1], inc=se)
            for half in range(2):
                p.op("tensor", lambda e, half=half, nt=nt, res=res: e.matmul(
                    s1_ps[:, 512 * half:512 * half + 512], ones_cf[:, :],
                    res[:, 512 * half:512 * half + 512],
                    start=(nt == 0), stop=(nt == NKT - 1)),
                    waits=[v_x1], inc=te)
            for half in range(2):
                p.op("tensor", lambda e, half=half, nt=nt: e.matmul(
                    s2_ps[:, 512 * half:512 * half + 512], ones_cf[:, :],
                    xsq[:, 512 * half:512 * half + 512],
                    start=(nt == 0), stop=(nt == NKT - 1)),
                    waits=[(se, se.v)], inc=te)
            p.op("sync", lambda e, nt=nt, res=res: e.dma_start(
                out=x1_d[128 * nt:128 * nt + 128, :], in_=res[:, :]),
                waits=[(te, te.v)], inc=so, n=16)
        v_st1 = (te, te.v)

        v_rows1 = ln_rows(p, ctrs, ln_tiles, s1_ps, s2_ps, v_st1)
        v_ln1 = materialize_pass(x1_d, v_rows1,
                                 extra_waits=[(so0, so0.v), (so1, so1.v)])

        lgps = yA[0:4, :]
        for k in range(NKT):
            for half in range(2):
                p.op("tensor", lambda e, k=k, half=half: e.matmul(
                    lgps[:, 512 * half:512 * half + 512],
                    wr_bf[:, 4 * k:4 * k + 4],
                    ln_sb[:, TPC * k + 512 * half:TPC * k + 512 * half + 512],
                    start=(k == 0), stop=(k == NKT - 1)),
                    waits=[v_ln1, v_consts] if k == 0 and half == 0 else [],
                    inc=te if (k == NKT - 1 and half == 1) else None)
        p.op("vector", lambda e: e.tensor_copy(lg_sb[:, :], lgps[:, :]),
             waits=[(te, te.v)], inc=ve)

        for tc in range(8):
            v_in = (ve, ve.v)
            p.op("tensor", lambda e, tc=tc: e.transpose(
                yB[:, 0:4], lg_sb[:, 128 * tc:128 * tc + 128], id_f[0:4, 0:4]),
                waits=[v_in], inc=te)
            p.op("vector", lambda e: e.tensor_copy(lgT_sb[:, :], yB[:, 0:4]),
                 waits=[(te, te.v)], inc=ve)
            p.op("vector", lambda e: e.reduce_max(
                g_m1[:, :], lgT_sb[:, :], axis=AX.X), waits=[(ve, ve.v)], inc=ve)
            p.op("vector", lambda e: e.tensor_scalar(
                g_nm1[:, :], g_m1[:, :], -1.0, None, op0=ALU.mult),
                waits=[(ve, ve.v)], inc=ve)
            p.op("vector", lambda e: e.tensor_scalar(
                g_eq[:, :], lgT_sb[:, :], g_m1[:, :], 1e30,
                op0=ALU.is_equal, op1=ALU.mult), waits=[(ve, ve.v)], inc=ve)
            p.op("vector", lambda e: e.tensor_sub(
                g_t2[:, :], lgT_sb[:, :], g_eq[:, :]), waits=[(ve, ve.v)], inc=ve)
            p.op("vector", lambda e: e.reduce_max(
                g_m2[:, :], g_t2[:, :], axis=AX.X), waits=[(ve, ve.v)], inc=ve)
            p.op("vector", lambda e: e.tensor_scalar(
                g_mask[:, :], lgT_sb[:, :], g_m2[:, :], None,
                op0=ALU.is_ge), waits=[(ve, ve.v)], inc=ve)
            p.op("scalar", lambda e: e.activation(
                g_e[:, :], lgT_sb[:, :], AF.Exp, bias=g_nm1[:, :]),
                waits=[(ve, ve.v)], inc=se)
            p.op("vector", lambda e: e.tensor_mul(
                g_e[:, :], g_e[:, :], g_mask[:, :]),
                waits=[(se, se.v)], inc=ve)
            p.op("vector", lambda e: e.reduce_sum(
                g_den[:, :], g_e[:, :], axis=AX.X), waits=[(ve, ve.v)], inc=ve)
            p.op("vector", lambda e: e.reciprocal(
                g_rec[:, :], g_den[:, :]), waits=[(ve, ve.v)], inc=ve)
            p.op("vector", lambda e: e.tensor_scalar(
                g_out[:, :], g_e[:, :], g_rec[:, :], SCALE_MOE,
                op0=ALU.mult, op1=ALU.mult), waits=[(ve, ve.v)], inc=ve)
            p.op("tensor", lambda e: e.transpose(
                yB[0:4, 0:128], g_out[:, :], id_f[:, :]),
                waits=[(ve, ve.v)], inc=te)
            p.op("vector", lambda e, tc=tc: e.tensor_copy(
                grow[:, 128 * tc:128 * tc + 128], yB[0:4, 0:128]),
                waits=[(te, te.v)], inc=ve)
        v_gates = (ve, ve.v)
        gE_ps = yA[0:64, :]
        for half in range(2):
            p.op("tensor", lambda e, half=half: e.matmul(
                gE_ps[:, 512 * half:512 * half + 512], E16t[:, :],
                grow[:, 512 * half:512 * half + 512], start=True, stop=True),
                waits=[v_gates], inc=te)
        p.op("vector", lambda e: e.tensor_copy(gE_sb[:, :], gE_ps[:, :]),
             waits=[(te, te.v)], inc=ve)
        v_gE = (ve, ve.v)

        midps = s1_ps_full[0:64, :]
        for k in range(NKT):
            for half in range(2):
                p.op("tensor", lambda e, k=k, half=half: e.matmul(
                    midps[:, 512 * half:512 * half + 512],
                    ar_bf[:, 64 * k:64 * k + 64],
                    ln_sb[:, TPC * k + 512 * half:TPC * k + 512 * half + 512],
                    start=(k == 0), stop=(k == NKT - 1)),
                    waits=[v_consts] if k == 0 and half == 0 else [],
                    inc=te if (k == NKT - 1 and half == 1) else None)
        p.op("vector", lambda e: e.tensor_mul(
            mid_sb[:, :], midps[:, :], gE_sb[:, :]),
            waits=[(te, te.v), v_gE], inc=ve)
        v_mid = (ve, ve.v)
        for ot in range(NKT):
            dl = yA if ot % 2 == 0 else yB
            so = so0 if ot % 2 == 0 else so1
            xl = xl0 if ot % 2 == 0 else xl1
            xt = xt0 if ot % 2 == 0 else xt1
            res = res0 if ot % 2 == 0 else res1
            vx = xl.v + 16
            p.op("sync", lambda e, ot=ot, xt=xt: e.dma_start(
                out=xt[:, :], in_=x1_d[128 * ot:128 * ot + 128, :]),
                waits=[(ve, ve.v)], inc=xl, n=16)
            for half in range(2):
                p.op("tensor", lambda e, ot=ot, half=half, dl=dl: e.matmul(
                    dl[:, 512 * half:512 * half + 512],
                    br_bf[:, 128 * ot:128 * ot + 128],
                    mid_sb[:, 512 * half:512 * half + 512],
                    start=True, stop=True),
                    waits=[v_mid, (ve, ve.v)] if half == 0 else [],
                    inc=te if half == 1 else None)
            p.op("vector", lambda e, dl=dl, xt=xt, res=res: e.tensor_add(
                res[:, :], dl[:, :], xt[:, :]),
                waits=[(te, te.v), (xl, vx), (so, so.v)], inc=ve)
            p.op("sync", lambda e, ot=ot, res=res: e.dma_start(
                out=x2_o[128 * ot:128 * ot + 128, :], in_=res[:, :]),
                waits=[(ve, ve.v)], inc=so, n=16)
        p.barrier("sync", [(so0, so0.v), (so1, so1.v)])
        p.run()
    return nc


def l0_in_maps(inputs, cs, x=None):
    X = inputs["L0_attn_data"].reshape(8192, D) if x is None else x
    xT = np.ascontiguousarray(X.T.astype(np.float32))
    A = inputs["L0_ffn_A"]
    a_re = np.ascontiguousarray(A.reshape(64, D).T)
    b_re = np.ascontiguousarray(inputs["L0_ffn_B"].transpose(0, 2, 1).reshape(64, D))
    wrT = np.ascontiguousarray(inputs["L0_ffn_Wr"].T)
    b0 = np.ascontiguousarray(inputs["L0_attn_bias"].reshape(32, 128).T)
    maps = []
    for c in range(NCORE):
        maps.append({
            "xT_s": np.ascontiguousarray(xT[:, TPC * c:TPC * (c + 1)]),
            "wmT": None,
            "b0": b0, "a_re": a_re, "b_re": b_re, "wrT": wrT,
            "E16": cs["E16"], "ident_f": cs["ident_f"],
            "onesrow_f": cs["onesrow_f"], "onescol_f": cs["onescol_f"],
        })
    return maps


def run_l0(inputs, wmT):
    cs = consts_host()
    maps = l0_in_maps(inputs, cs)
    for m in maps:
        m["wmT"] = wmT
    nc = build_l0()
    res = run_bass_kernel_spmd(nc, maps, core_ids=list(range(NCORE)))
    return np.concatenate([res.results[c]["x2_o"] for c in range(NCORE)], axis=1)



def build_qkv():
    nc = bass.Bass("TRN2", target_bir_lowering=False, debug=False)
    d = lambda name, shape, dt, kind: nc.dram_tensor(name, shape, dt, kind=kind).ap()
    xT = d("x2_s", [D, TPC], F32, "ExternalInput")
    wq = d("wqT_b", [D, D], BF16, "ExternalInput")
    wk = d("wkT_b", [D, D], BF16, "ExternalInput")
    wv = d("wvT_b", [D, D], BF16, "ExternalInput")
    bq = d("bq", [128, 32], F32, "ExternalInput")
    bk = d("bk", [128, 32], F32, "ExternalInput")
    bv = d("bv", [128, 32], F32, "ExternalInput")
    onesrow_f = d("onesrow_f", [1, 128], F32, "ExternalInput")
    onescol_f = d("onescol_f", [128, 1], F32, "ExternalInput")
    qo = d("qT_o", [D, TPC], BF16, "ExternalOutput")
    ko = d("kT_o", [D, TPC], BF16, "ExternalOutput")
    vo = d("vT_o", [D, TPC], BF16, "ExternalOutput")

    p = Prog(nc)
    with ExitStack() as ctx:
        sb = lambda name, shape, dt: ctx.enter_context(nc.sbuf_tensor(name, shape, dt))
        psum = lambda name, shape: ctx.enter_context(nc.psum_tensor(name, shape, F32))
        C = lambda name: Ctr(ctx.enter_context(nc.semaphore(name)))
        ctrs = {n: C(n) for n in ("ve", "se", "te")}
        ve, se, te = ctrs["ve"], ctrs["se"], ctrs["te"]
        cl = C("cl"); xl0 = C("xl0"); xl1 = C("xl1")
        wl0 = C("wl0"); wl1 = C("wl1"); so0 = C("so0"); so1 = C("so1")

        ones_rf = sb("ones_rf", [1, 128], F32)
        ones_cf = sb("ones_cf", [128, 1], F32)
        bias_sb = sb("bias_sb", [128, 3 * 32], F32)
        xt0 = sb("xt0", [128, TPC], F32)
        xt1 = sb("xt1", [128, TPC], F32)
        xsq = sb("xsq", [128, TPC], F32)
        m_row = sb("m_row", [1, TPC], F32)
        var_row = sb("var_row", [1, TPC], F32)
        rstd_row = sb("rstd_row", [1, TPC], F32)
        mB = sb("mB", [128, TPC], F32)
        rstdB = sb("rstdB", [128, TPC], F32)
        ln_sb = sb("ln_sb", [128, NKT * TPC], BF16)
        wslab0 = sb("wslab0", [128, NKT * 128], BF16)
        wslab1 = sb("wslab1", [128, NKT * 128], BF16)
        out0 = sb("out0", [128, TPC], BF16)
        out1 = sb("out1", [128, TPC], BF16)

        s1_ps_full = psum("s1_ps", [64, TPC])
        s2_ps_full = psum("s2_ps", [64, TPC])
        s1_ps = s1_ps_full[0:1, :]
        s2_ps = s2_ps_full[0:1, :]
        yA = psum("yA", [128, TPC])
        yB = psum("yB", [128, TPC])

        for dst, srcd in [(ones_rf, onesrow_f), (ones_cf, onescol_f)]:
            p.op("sync", lambda e, dst=dst, srcd=srcd: e.dma_start(
                out=dst[:, :], in_=srcd), inc=cl, n=16)
        for i, bd in enumerate([bq, bk, bv]):
            p.op("sync", lambda e, i=i, bd=bd: e.dma_start(
                out=bias_sb[:, 32 * i:32 * i + 32], in_=bd), inc=cl, n=16)
        v_consts = (cl, cl.v)

        ln_tiles = dict(m_row=m_row, var_row=var_row, rstd_row=rstd_row,
                        mB=mB, rstdB=rstdB, bc_ps=yA, ones_rf=ones_rf)

        for kt in range(NKT):
            xl = xl0 if kt % 2 == 0 else xl1
            xt = xt0 if kt % 2 == 0 else xt1
            vx = xl.v + 16
            p.op("sync", lambda e, kt=kt, xt=xt: e.dma_start(
                out=xt[:, :], in_=xT[128 * kt:128 * kt + 128, :]),
                waits=[(te, te.v), (se, se.v)], inc=xl, n=16)
            p.op("scalar", lambda e, xt=xt: e.activation(
                xsq[:, :], xt[:, :], AF.Square),
                waits=[(xl, vx), (te, te.v)], inc=se)
            for half in range(2):
                p.op("tensor", lambda e, xt=xt, half=half, kt=kt: e.matmul(
                    s1_ps[:, 512 * half:512 * half + 512], ones_cf[:, :],
                    xt[:, 512 * half:512 * half + 512],
                    start=(kt == 0), stop=(kt == NKT - 1)),
                    waits=[(xl, vx)] + ([v_consts] if kt == 0 else []), inc=te)
            for half in range(2):
                p.op("tensor", lambda e, half=half, kt=kt: e.matmul(
                    s2_ps[:, 512 * half:512 * half + 512], ones_cf[:, :],
                    xsq[:, 512 * half:512 * half + 512],
                    start=(kt == 0), stop=(kt == NKT - 1)),
                    waits=[(se, se.v)], inc=te)
        v_st = (te, te.v)
        v_rows = ln_rows(p, ctrs, ln_tiles, s1_ps, s2_ps, v_st)
        for kt in range(NKT):
            xl = xl0 if kt % 2 == 0 else xl1
            xt = xt0 if kt % 2 == 0 else xt1
            vx = xl.v + 16
            p.op("sync", lambda e, kt=kt, xt=xt: e.dma_start(
                out=xt[:, :], in_=xT[128 * kt:128 * kt + 128, :]),
                waits=[(ve, ve.v), (se, se.v)], inc=xl, n=16)
            p.op("vector", lambda e, xt=xt: e.tensor_sub(
                xsq[:, :], xt[:, :], mB[:, :]),
                waits=[(xl, vx), v_rows, (se, se.v)], inc=ve)
            p.op("vector", lambda e, kt=kt: e.tensor_mul(
                ln_sb[:, TPC * kt:TPC * kt + TPC], xsq[:, :], rstdB[:, :]),
                waits=[(ve, ve.v)], inc=ve)
        v_ln = (ve, ve.v)

        last_y = {0: 0, 1: 0}
        for wi, (wd, od) in enumerate([(wq, qo), (wk, ko), (wv, vo)]):
            for nt in range(NKT):
                i = wi * NKT + nt
                par = i % 2
                wl = wl0 if par == 0 else wl1
                ws = wslab0 if par == 0 else wslab1
                y = yA if par == 0 else yB
                so = so0 if par == 0 else so1
                ob = out0 if par == 0 else out1
                vw = wl.v + 16
                p.op("sync", lambda e, nt=nt, ws=ws, wd=wd: e.dma_start(
                    out=ws[:, :].rearrange("p (kt c) -> p kt c", c=128),
                    in_=wd.rearrange("(kt p) n -> p kt n", p=128)[
                        :, :, 128 * nt:128 * nt + 128]),
                    waits=[(te, last_y[par])], inc=wl, n=16)
                for k in range(NKT):
                    for half in range(2):
                        p.op("tensor", lambda e, k=k, half=half, ws=ws, y=y: e.matmul(
                            y[:, 512 * half:512 * half + 512],
                            ws[:, 128 * k:128 * k + 128],
                            ln_sb[:, TPC * k + 512 * half:TPC * k + 512 * half + 512],
                            start=(k == 0), stop=(k == NKT - 1)),
                            waits=([(wl, vw), v_ln, (ve, ve.v)]
                                   if k == 0 and half == 0 else []),
                            inc=te if (k == NKT - 1 and half == 1) else None)
                last_y[par] = te.v
                p.op("vector", lambda e, wi=wi, nt=nt, y=y, ob=ob: e.tensor_scalar(
                    ob[:, :], y[:, :], bias_sb[:, 32 * wi + nt:32 * wi + nt + 1],
                    None, op0=ALU.add),
                    waits=[(te, te.v), (so, so.v)], inc=ve)
                p.op("sync", lambda e, nt=nt, od=od, ob=ob: e.dma_start(
                    out=od[128 * nt:128 * nt + 128, :], in_=ob[:, :]),
                    waits=[(ve, ve.v)], inc=so, n=16)
        p.barrier("sync", [(so0, so0.v), (so1, so1.v)])
        p.run()
    return nc


def run_qkv(inputs, x2T, wqT_b, wkT_b, wvT_b):
    cs = consts_host()
    maps = []
    for c in range(NCORE):
        maps.append({
            "x2_s": np.ascontiguousarray(x2T[:, TPC * c:TPC * (c + 1)]).astype(np.float32),
            "wqT_b": wqT_b, "wkT_b": wkT_b, "wvT_b": wvT_b,
            "bq": np.ascontiguousarray(inputs["L1_attn_bq"].reshape(32, 128).T),
            "bk": np.ascontiguousarray(inputs["L1_attn_bk"].reshape(32, 128).T),
            "bv": np.ascontiguousarray(inputs["L1_attn_bv"].reshape(32, 128).T),
            "onesrow_f": cs["onesrow_f"], "onescol_f": cs["onescol_f"],
        })
    nc = build_qkv()
    res = run_bass_kernel_spmd(nc, maps, core_ids=list(range(NCORE)))
    g = lambda k: np.concatenate([res.results[c][k] for c in range(NCORE)], axis=1)
    return g("qT_o"), g("kT_o"), g("vT_o")



def build_attn():
    nc = bass.Bass("TRN2", target_bir_lowering=False, debug=False)
    d = lambda name, shape, dt, kind: nc.dram_tensor(name, shape, dt, kind=kind).ap()
    qT = d("qT_s", [D, TPC], BF16, "ExternalInput")
    kT = d("kT_s", [D, TPC], BF16, "ExternalInput")
    vT = d("vT_s", [D, TPC], BF16, "ExternalInput")
    xT = d("x2_s", [D, TPC], F32, "ExternalInput")
    identd = d("ident_bf", [128, 128], BF16, "ExternalInput")
    x3_o = d("x3_o", [D, TPC], F32, "ExternalOutput")
    HT = 512

    p = Prog(nc)
    with ExitStack() as ctx:
        sb = lambda name, shape, dt: ctx.enter_context(nc.sbuf_tensor(name, shape, dt))
        C = lambda name: Ctr(ctx.enter_context(nc.semaphore(name)))
        ve, se, te = C("ve"), C("se"), C("te")
        cl = C("cl"); ql = C("ql"); kl = C("kl"); vl = C("vl")
        xl0 = C("xl0"); xl1 = C("xl1"); so0 = C("so0"); so1 = C("so1")

        id_bf = sb("id_bf", [128, 128], BF16)
        q_sb = sb("q_sb", [128, 32 * HT], BF16)
        k_sb = sb("k_sb", [128, 32 * HT], BF16)
        v_sb = sb("v_sb", [128, 32 * HT], BF16)
        ctx_sb = sb("ctx_sb", [128, 32 * HT], F32)
        sc_m = sb("sc_m", [128, 1], F32)
        sc_nm = sb("sc_nm", [128, 1], F32)
        sc_e = sb("sc_e", [128, 32], F32)
        sc_s = sb("sc_s", [128, 1], F32)
        sc_r = sb("sc_r", [128, 1], F32)
        probs = sb("probs", [128, 32], BF16)
        ptblk = sb("ptblk", [128, 128], BF16)
        vt0 = sb("vt0", [128, 128], BF16)
        vt1 = sb("vt1", [128, 128], BF16)
        xt0 = sb("xt0", [128, HT], F32)
        xt1 = sb("xt1", [128, HT], F32)
        res0 = sb("res0", [128, HT], F32)
        res1 = sb("res1", [128, HT], F32)
        ps_sc0 = ctx.enter_context(nc.psum_tensor("ps_sc0", [128, 32], F32))
        ps_sc1 = ctx.enter_context(nc.psum_tensor("ps_sc1", [128, 32], F32))
        ps_vt0 = ctx.enter_context(nc.psum_tensor("ps_vt0", [128, 128], BF16))
        ps_vt1 = ctx.enter_context(nc.psum_tensor("ps_vt1", [128, 128], BF16))
        ps_cx0 = ctx.enter_context(nc.psum_tensor("ps_cx0", [128, 128], F32))
        ps_cx1 = ctx.enter_context(nc.psum_tensor("ps_cx1", [128, 128], F32))

        p.op("sync", lambda e: e.dma_start(out=id_bf[:, :], in_=identd), inc=cl, n=16)
        p.op("vector", lambda e: e.memset(ptblk[:, :], 0.0), inc=ve)
        v_consts = (cl, cl.v)
        v_pt0 = (ve, ve.v)

        def tok_view(sbx, t):
            return sbx[:, :].rearrange("p (h t) -> p h t", t=HT)[:, :, t]

        last_exp = {0: 0, 1: 0}
        last_vtc = {0: 0, 1: 0}
        last_cxe = {0: 0, 1: 0}
        last_pv = {0: 0, 1: 0}

        for half in range(2):
            vq = ql.v + 16; vk = kl.v + 16; vv = vl.v + 16
            for (dr, sbx, ct, vx) in [(qT, q_sb, ql, vq), (kT, k_sb, kl, vk),
                                      (vT, v_sb, vl, vv)]:
                p.op("sync", lambda e, dr=dr, sbx=sbx, half=half: e.dma_start(
                    out=sbx[:, :].rearrange("p (h t) -> p h t", t=HT),
                    in_=dr.rearrange("(h p) t -> p h t", p=128)[
                        :, :, HT * half:HT * half + HT]),
                    waits=[(te, te.v), (ve, ve.v)], inc=ct, n=16)
            for g in range(HT // 4):
                par = g % 2
                ps_sc = ps_sc0 if par == 0 else ps_sc1
                ps_vt = ps_vt0 if par == 0 else ps_vt1
                ps_cx = ps_cx0 if par == 0 else ps_cx1
                vt = vt0 if par == 0 else vt1
                for jj in range(4):
                    t0 = 4 * g + jj
                    p.op("tensor", lambda e, t0=t0, jj=jj, ps_sc=ps_sc: e.matmul(
                        ps_sc[32 * jj:32 * jj + 32, :],
                        tok_view(q_sb, t0), tok_view(k_sb, t0),
                        start=True, stop=True, tile_position=(0, 32 * jj)),
                        waits=([(ql, vq), (kl, kl.v), (se, last_exp[par])]
                               if jj == 0 else []),
                        inc=te if jj == 3 else None)
                v_qk = (te, te.v)
                for jj in range(4):
                    t0 = 4 * g + jj
                    p.op("tensor", lambda e, t0=t0, jj=jj, ps_vt=ps_vt: e.transpose(
                        ps_vt[32 * jj:32 * jj + 32, :],
                        tok_view(v_sb, t0), id_bf[:, :],
                        tile_position=(0, 32 * jj)),
                        waits=([(vl, vv), v_consts, (ve, last_vtc[par])]
                               if jj == 0 else []),
                        inc=te if jj == 3 else None)
                v_vt = (te, te.v)
                p.op("vector", lambda e, ps_sc=ps_sc: e.reduce_max(
                    sc_m[:, :], ps_sc[:, :], axis=AX.X),
                    waits=[v_qk, (ve, ve.v)], inc=ve)
                p.op("vector", lambda e: e.tensor_scalar(
                    sc_nm[:, :], sc_m[:, :], -INV_SQRT_HD, None, op0=ALU.mult),
                    waits=[(ve, ve.v)], inc=ve)
                p.op("scalar", lambda e, ps_sc=ps_sc: e.activation(
                    sc_e[:, :], ps_sc[:, :], AF.Exp, bias=sc_nm[:, :],
                    scale=INV_SQRT_HD),
                    waits=[(ve, ve.v)], inc=se)
                last_exp[par] = se.v
                p.op("vector", lambda e: e.reduce_sum(
                    sc_s[:, :], sc_e[:, :], axis=AX.X),
                    waits=[(se, se.v)], inc=ve)
                p.op("vector", lambda e: e.reciprocal(
                    sc_r[:, :], sc_s[:, :]), waits=[(ve, ve.v)], inc=ve)
                p.op("vector", lambda e: e.tensor_scalar(
                    probs[:, :], sc_e[:, :], sc_r[:, :], None, op0=ALU.mult),
                    waits=[(ve, ve.v)], inc=ve)
                for jj in range(4):
                    p.op("vector", lambda e, jj=jj: e.transpose(
                        ptblk[32 * jj:32 * jj + 32, 32 * jj:32 * jj + 32],
                        probs[32 * jj:32 * jj + 32, :]),
                        waits=([(ve, ve.v), v_pt0, (te, last_pv[0]),
                                (te, last_pv[1])] if jj == 0 else []),
                        inc=ve if jj == 3 else None)
                v_pt = (ve, ve.v)
                p.op("vector", lambda e, ps_vt=ps_vt, vt=vt: e.tensor_copy(
                    vt[:, :], ps_vt[:, :]),
                    waits=[v_vt, (te, last_pv[par])], inc=ve)
                last_vtc[par] = ve.v
                v_vte = (ve, ve.v)
                p.op("tensor", lambda e, vt=vt, ps_cx=ps_cx: e.matmul(
                    ps_cx[:, :], vt[:, :], ptblk[:, :], start=True, stop=True),
                    waits=[v_pt, v_vte, (ve, last_cxe[par])], inc=te)
                last_pv[par] = te.v
                p.op("vector", lambda e, g=g, ps_cx=ps_cx: e.tensor_copy(
                    ctx_sb[:, :].rearrange("p (h t) -> p t h", t=HT)[
                        :, 4 * g:4 * g + 4, :],
                    ps_cx[:, :].rearrange("p (t h) -> p t h", h=32)),
                    waits=[(te, te.v)], inc=ve)
                last_cxe[par] = ve.v
            v_attn = (ve, ve.v)
            for kt in range(NKT):
                par = kt % 2
                xl = xl0 if par == 0 else xl1
                xt = xt0 if par == 0 else xt1
                res = res0 if par == 0 else res1
                so = so0 if par == 0 else so1
                vx = xl.v + 16
                p.op("sync", lambda e, kt=kt, xt=xt, half=half: e.dma_start(
                    out=xt[:, :],
                    in_=xT[128 * kt:128 * kt + 128, HT * half:HT * half + HT]),
                    waits=[(ve, ve.v)], inc=xl, n=16)
                p.op("vector", lambda e, kt=kt, xt=xt, res=res: e.tensor_add(
                    res[:, :], ctx_sb[:, HT * kt:HT * kt + HT], xt[:, :]),
                    waits=[v_attn, (xl, vx), (so, so.v)], inc=ve)
                p.op("sync", lambda e, kt=kt, res=res, half=half: e.dma_start(
                    out=x3_o[128 * kt:128 * kt + 128, HT * half:HT * half + HT],
                    in_=res[:, :]),
                    waits=[(ve, ve.v)], inc=so, n=16)
        p.barrier("sync", [(so0, so0.v), (so1, so1.v)])
        p.run()
    return nc


def run_attn(x2T, qT, kT, vT):
    cs = consts_host()
    ident_bf = np.eye(128, dtype=BF)
    maps = []
    for c in range(NCORE):
        sl = slice(TPC * c, TPC * (c + 1))
        maps.append({
            "qT_s": np.ascontiguousarray(qT[:, sl]),
            "kT_s": np.ascontiguousarray(kT[:, sl]),
            "vT_s": np.ascontiguousarray(vT[:, sl]),
            "x2_s": np.ascontiguousarray(x2T[:, sl]).astype(np.float32),
            "ident_bf": ident_bf,
        })
    nc = build_attn()
    res = run_bass_kernel_spmd(nc, maps, core_ids=list(range(NCORE)))
    return np.concatenate([res.results[c]["x3_o"] for c in range(NCORE)], axis=1)



def build_ffn():
    nc = bass.Bass("TRN2", target_bir_lowering=False, debug=False)
    d = lambda name, shape, dt, kind: nc.dram_tensor(name, shape, dt, kind=kind).ap()
    xT = d("x3_s", [D, TPC], F32, "ExternalInput")
    wgu = d("wgu", [D, GU], BF16, "ExternalInput")
    wdn = d("wdn", [INTER, D], BF16, "ExternalInput")
    onesrow_f = d("onesrow_f", [1, 128], F32, "ExternalInput")
    onescol_f = d("onescol_f", [128, 1], F32, "ExternalInput")
    out_o = d("out_o", [D, TPC], F32, "ExternalOutput")
    HT = 512
    NI = 86

    p = Prog(nc)
    with ExitStack() as ctx:
        sb = lambda name, shape, dt: ctx.enter_context(nc.sbuf_tensor(name, shape, dt))
        psum = lambda name, shape: ctx.enter_context(nc.psum_tensor(name, shape, F32))
        C = lambda name: Ctr(ctx.enter_context(nc.semaphore(name)))
        ctrs = {n: C(n) for n in ("ve", "se", "te")}
        ve, se, te = ctrs["ve"], ctrs["se"], ctrs["te"]
        cl = C("cl"); xl0 = C("xl0"); xl1 = C("xl1")
        gl0 = C("gl0"); gl1 = C("gl1"); ul0 = C("ul0"); ul1 = C("ul1")
        dl0 = C("dl0"); dl1 = C("dl1"); so0 = C("so0"); so1 = C("so1")

        ones_rf = sb("ones_rf", [1, 128], F32)
        ones_cf = sb("ones_cf", [128, 1], F32)
        xt0 = sb("xt0", [128, HT], F32)
        xt1 = sb("xt1", [128, HT], F32)
        xsq = sb("xsq", [128, HT], F32)
        m_row = sb("m_row", [1, TPC], F32)
        rstd_row = sb("rstd_row", [1, TPC], F32)
        var_row = rstd_row
        mB = sb("mB", [128, HT], F32)
        rstdB = sb("rstdB", [128, HT], F32)
        ln_sb = sb("ln_sb", [128, NKT * HT], BF16)
        gs0 = sb("gs0", [128, NKT * 128], BF16)
        gs1 = sb("gs1", [128, NKT * 128], BF16)
        us0 = sb("us0", [128, NKT * 128], BF16)
        us1 = sb("us1", [128, NKT * 128], BF16)
        silu_sb = sb("silu_sb", [128, 512], BF16)
        h_sb = sb("h_sb", [128, NI * HT], BF16)
        ds0 = sb("ds0", [128, NI * 128], BF16)
        res0 = sb("res0", [128, HT], F32)
        res1 = sb("res1", [128, HT], F32)

        s1_ps = psum("s1_ps", [1, TPC])
        s2_ps = psum("s2_ps", [1, TPC])
        P1 = psum("P1", [128, TPC])
        P2 = psum("P2", [128, TPC])

        for dst, srcd in [(ones_rf, onesrow_f), (ones_cf, onescol_f)]:
            p.op("sync", lambda e, dst=dst, srcd=srcd: e.dma_start(
                out=dst[:, :], in_=srcd), inc=cl, n=16)
        v_consts = (cl, cl.v)

        ln_tiles = dict(m_row=m_row, var_row=var_row, rstd_row=rstd_row,
                        mB=mB, rstdB=rstdB, bc_ps=P1, ones_rf=ones_rf)

        for u in range(2 * NKT):
            kt, h2 = divmod(u, 2)
            xl = xl0 if u % 2 == 0 else xl1
            xt = xt0 if u % 2 == 0 else xt1
            vx = xl.v + 16
            p.op("sync", lambda e, kt=kt, h2=h2, xt=xt: e.dma_start(
                out=xt[:, :],
                in_=xT[128 * kt:128 * kt + 128, 512 * h2:512 * h2 + 512]),
                waits=[(te, te.v), (se, se.v)], inc=xl, n=16)
            p.op("scalar", lambda e, xt=xt: e.activation(
                xsq[:, :], xt[:, :], AF.Square),
                waits=[(xl, vx), (te, te.v)], inc=se)
            p.op("tensor", lambda e, xt=xt, h2=h2, u=u: e.matmul(
                s1_ps[:, 512 * h2:512 * h2 + 512], ones_cf[:, :], xt[:, :],
                start=(u < 2), stop=(u >= 2 * NKT - 2)),
                waits=[(xl, vx)] + ([v_consts] if u == 0 else []), inc=te)
            p.op("tensor", lambda e, h2=h2, u=u: e.matmul(
                s2_ps[:, 512 * h2:512 * h2 + 512], ones_cf[:, :], xsq[:, :],
                start=(u < 2), stop=(u >= 2 * NKT - 2)),
                waits=[(se, se.v)], inc=te)
        v_st = (te, te.v)
        v_rows = ln_rows(p, ctrs, ln_tiles, s1_ps, s2_ps, v_st, bcast=False)

        last_gu = {0: 0, 1: 0}
        last_dn = {0: 0, 1: 0}
        for half in range(2):
            p.op("tensor", lambda e, half=half: e.matmul(
                P1[:, 0:512], ones_rf[:, :], m_row[:, HT * half:HT * half + 512],
                start=True, stop=True),
                waits=[v_rows, (te, te.v), (ve, ve.v)], inc=te)
            p.op("tensor", lambda e, half=half: e.matmul(
                P1[:, 512:1024], ones_rf[:, :],
                rstd_row[:, HT * half:HT * half + 512],
                start=True, stop=True), inc=te)
            p.op("vector", lambda e: e.tensor_copy(mB[:, :], P1[:, 0:512]),
                 waits=[(te, te.v)], inc=ve)
            p.op("vector", lambda e: e.tensor_copy(rstdB[:, :], P1[:, 512:1024]),
                 waits=[(te, te.v)], inc=ve)
            v_bc = (ve, ve.v)
            for kt in range(NKT):
                xl = xl0 if kt % 2 == 0 else xl1
                xt = xt0 if kt % 2 == 0 else xt1
                vx = xl.v + 16
                p.op("sync", lambda e, kt=kt, xt=xt, half=half: e.dma_start(
                    out=xt[:, :],
                    in_=xT[128 * kt:128 * kt + 128, HT * half:HT * half + HT]),
                    waits=[(ve, ve.v), (se, se.v)], inc=xl, n=16)
                p.op("vector", lambda e, xt=xt: e.tensor_sub(
                    xsq[:, :], xt[:, :], mB[:, :]),
                    waits=[(xl, vx), v_bc, (se, se.v)], inc=ve)
                p.op("vector", lambda e, kt=kt: e.tensor_mul(
                    ln_sb[:, HT * kt:HT * kt + HT], xsq[:, :], rstdB[:, :]),
                    waits=[(ve, ve.v)], inc=ve)
            v_ln = (ve, ve.v)

            for pt in range(NI):
                par = pt % 2
                gs = gs0 if par == 0 else gs1
                us = us0 if par == 0 else us1
                glc = gl0 if par == 0 else gl1
                ulc = ul0 if par == 0 else ul1
                gps = (P1 if par == 0 else P2)[:, 0:512]
                ups = (P1 if par == 0 else P2)[:, 512:1024]
                vg = glc.v + 16
                vu = ulc.v + 16
                p.op("sync", lambda e, pt=pt, gs=gs: e.dma_start(
                    out=gs[:, :].rearrange("p (kt c) -> p kt c", c=128),
                    in_=wgu.rearrange("(kt p) n -> p kt n", p=128)[
                        :, :, 128 * pt:128 * pt + 128]),
                    waits=[(te, last_gu[par])], inc=glc, n=16)
                p.op("sync", lambda e, pt=pt, us=us: e.dma_start(
                    out=us[:, :].rearrange("p (kt c) -> p kt c", c=128),
                    in_=wgu.rearrange("(kt p) n -> p kt n", p=128)[
                        :, :, INTER + 128 * pt:INTER + 128 * pt + 128]),
                    waits=[(te, last_gu[par])], inc=ulc, n=16)
                for k in range(NKT):
                    p.op("tensor", lambda e, k=k, gs=gs, gps=gps: e.matmul(
                        gps[:, :], gs[:, 128 * k:128 * k + 128],
                        ln_sb[:, HT * k:HT * k + HT],
                        start=(k == 0), stop=(k == NKT - 1)),
                        waits=([(glc, vg), v_ln, (ve, ve.v), (se, se.v)]
                               if k == 0 else []),
                        inc=None)
                    p.op("tensor", lambda e, k=k, us=us, ups=ups: e.matmul(
                        ups[:, :], us[:, 128 * k:128 * k + 128],
                        ln_sb[:, HT * k:HT * k + HT],
                        start=(k == 0), stop=(k == NKT - 1)),
                        waits=[(ulc, vu)] if k == 0 else [],
                        inc=te if k == NKT - 1 else None)
                last_gu[par] = te.v
                v_gu = (te, te.v)
                p.op("scalar", lambda e, gps=gps: e.activation(
                    silu_sb[:, :], gps[:, :], AF.Silu),
                    waits=[v_gu], inc=se)
                p.op("vector", lambda e, pt=pt, ups=ups: e.tensor_mul(
                    h_sb[:, HT * pt:HT * pt + HT], ups[:, :], silu_sb[:, :]),
                    waits=[(se, se.v)], inc=ve)
            v_h = (ve, ve.v)

            for nt in range(NKT):
                par = nt % 2
                ds = ds0
                dlc = dl0
                dps = (P1 if par == 0 else P2)[:, 0:512]
                so = so0 if par == 0 else so1
                res = res0 if par == 0 else res1
                xl = xl0 if par == 0 else xl1
                xt = xt0 if par == 0 else xt1
                vd = dlc.v + 32
                p.op("sync", lambda e, nt=nt, ds=ds: e.dma_start(
                    out=ds[:, 0:NI * 64].rearrange("p (kt c) -> p kt c", c=128),
                    in_=wdn.rearrange("(kt p) n -> p kt n", p=128)[
                        :, 0:43, 128 * nt:128 * nt + 128]),
                    waits=[(te, te.v)], inc=dlc, n=16)
                p.op("sync", lambda e, nt=nt, ds=ds: e.dma_start(
                    out=ds[:, NI * 64:NI * 128].rearrange("p (kt c) -> p kt c", c=128),
                    in_=wdn.rearrange("(kt p) n -> p kt n", p=128)[
                        :, 43:86, 128 * nt:128 * nt + 128]),
                    inc=dlc, n=16)
                vx = xl.v + 16
                p.op("sync", lambda e, nt=nt, xt=xt, half=half: e.dma_start(
                    out=xt[:, :],
                    in_=xT[128 * nt:128 * nt + 128, HT * half:HT * half + HT]),
                    waits=[(ve, ve.v)], inc=xl, n=16)
                for ic in range(NI):
                    p.op("tensor", lambda e, ic=ic, ds=ds, dps=dps: e.matmul(
                        dps[:, :], ds[:, 128 * ic:128 * ic + 128],
                        h_sb[:, HT * ic:HT * ic + HT],
                        start=(ic == 0), stop=(ic == NI - 1)),
                        waits=([(dlc, vd), v_h, (ve, ve.v)] if ic == 0 else []),
                        inc=te if ic == NI - 1 else None)
                last_dn[par] = te.v
                p.op("vector", lambda e, dps=dps, xt=xt, res=res: e.tensor_add(
                    res[:, :], dps[:, :], xt[:, :]),
                    waits=[(te, te.v), (xl, vx), (so, so.v)], inc=ve)
                p.op("sync", lambda e, nt=nt, res=res, half=half: e.dma_start(
                    out=out_o[128 * nt:128 * nt + 128, HT * half:HT * half + HT],
                    in_=res[:, :]),
                    waits=[(ve, ve.v)], inc=so, n=16)
        p.barrier("sync", [(so0, so0.v), (so1, so1.v)])
        p.run()
    return nc


def run_ffn(x3T, wgu, wdn):
    cs = consts_host()
    maps = []
    for c in range(NCORE):
        sl = slice(TPC * c, TPC * (c + 1))
        maps.append({
            "x3_s": np.ascontiguousarray(x3T[:, sl]).astype(np.float32),
            "wgu": wgu, "wdn": wdn,
            "onesrow_f": cs["onesrow_f"], "onescol_f": cs["onescol_f"],
        })
    nc = build_ffn()
    res = run_bass_kernel_spmd(nc, maps, core_ids=list(range(NCORE)))
    return np.concatenate([res.results[c]["out_o"] for c in range(NCORE)], axis=1)
